# version 06
# baseline (speedup 1.0000x reference)
import numpy as np
import concourse.bass as bass
import concourse.mybir as mybir
from concourse.bass_utils import run_bass_kernel_spmd

F32 = mybir.dt.float32
BF16 = mybir.dt.bfloat16
I32 = mybir.dt.int32
AF = mybir.ActivationFunctionType
OP = mybir.AluOpType
AX = mybir.AxisListType

B, S, D, H, DK, HID = 4, 4096, 1024, 16, 64, 3072
T = 2048
NQT = 4
EPS = 1e-5
DEQ = 1.0 / 127.0
DEQ2 = DEQ * DEQ

ENGS = ("sync", "gpsimd", "tensor", "scalar", "vector")
DMA_SEMS = ("ds", "dg")


class Sched:

    def __init__(self, limit=None):
        self.limit = limit
        self.total = 0
        self.q = {e: [] for e in ENGS}
        self.n = {s: 0 for s in ("ds", "dg", "pe", "act", "dve")}
        self.meta = {e: [] for e in ENGS}

    def check(self):
        ptr = {e: 0 for e in ENGS}
        cnt = {s: 0 for s in self.n}
        progress = True
        while progress:
            progress = False
            for e in ENGS:
                while ptr[e] < len(self.meta[e]):
                    waits, inc, tag = self.meta[e][ptr[e]]
                    if any(cnt[s] < t for (s, t) in waits):
                        break
                    if inc:
                        cnt[inc] += 16 if inc in DMA_SEMS else 1
                    ptr[e] += 1
                    progress = True
        stuck = {e: ptr[e] for e in ENGS if ptr[e] < len(self.meta[e])}
        if stuck:
            detail = {}
            for e in stuck:
                waits, inc, tag = self.meta[e][ptr[e]]
                detail[e] = (ptr[e], tag, [(s, t, cnt[s]) for (s, t) in waits if cnt[s] < t])
            raise RuntimeError(f"DEADLOCK: stuck steps {detail}")

    def step(self, eng, emit, waits=(), inc=None, tag=""):
        if self.limit is not None and self.total >= self.limit:
            return self.n.get(inc)
        self.total += 1
        waits = tuple((s, t) for (s, t) in waits if t)
        for s, t in waits:
            assert t <= self.n[s], f"wait target {s}={t} not yet emitted ({self.n[s]})"
        tgt = None
        if inc is not None:
            self.n[inc] += 16 if inc in DMA_SEMS else 1
            tgt = self.n[inc]

        def run(e, SM, waits=waits, emit=emit, inc=inc):
            for s, t in waits:
                e.wait_ge(SM[s], t)
            r = emit(e, SM)
            if inc is not None:
                assert r is not None, "inc requires a real instruction"
                r.then_inc(SM[inc], 16 if inc in DMA_SEMS else 1)
        self.q[eng].append(run)
        self.meta[eng].append((waits, inc, tag))
        return tgt


def build_nc(n_quads=4, n_heads=4, do_p3=True, do_p4=True, limit=None):
    nc = bass.Bass("TRN2", target_bir_lowering=False)

    x_in = nc.dram_tensor("x_in", [S, D], F32, kind="ExternalInput").ap()
    wq = nc.dram_tensor("wq", [D, D], F32, kind="ExternalInput").ap()
    wk = nc.dram_tensor("wk", [D, D], F32, kind="ExternalInput").ap()
    wv = nc.dram_tensor("wv", [D, D], F32, kind="ExternalInput").ap()
    wo = nc.dram_tensor("wo", [D, D], F32, kind="ExternalInput").ap()
    bq = nc.dram_tensor("bq", [D], F32, kind="ExternalInput").ap()
    bk = nc.dram_tensor("bk", [D], F32, kind="ExternalInput").ap()
    bv = nc.dram_tensor("bv", [D], F32, kind="ExternalInput").ap()
    bo = nc.dram_tensor("bo", [D], F32, kind="ExternalInput").ap()
    wg = nc.dram_tensor("wg", [HID, D], I32, kind="ExternalInput").ap()
    wu = nc.dram_tensor("wu", [HID, D], I32, kind="ExternalInput").ap()
    wd = nc.dram_tensor("wd", [D, HID], I32, kind="ExternalInput").ap()
    out = nc.dram_tensor("out", [T, D], F32, kind="ExternalOutput").ap()

    s_xbf = nc.dram_tensor("s_xbf", [S, D], BF16, kind="Internal").ap()
    s_wg = nc.dram_tensor("s_wg", [HID, D], BF16, kind="Internal").ap()
    s_wu = nc.dram_tensor("s_wu", [HID, D], BF16, kind="Internal").ap()
    s_wd = nc.dram_tensor("s_wd", [D, HID], BF16, kind="Internal").ap()
    s_x1b = nc.dram_tensor("s_x1b", [T, D], BF16, kind="Internal").ap()
    s_x1f = nc.dram_tensor("s_x1f", [T, D], F32, kind="Internal").ap()

    ident_dram = nc.inline_tensor(np.eye(128, dtype=np.float32), name="ident128")

    SBASE = 16512
    A = lambda name, shape, dt, off: nc.alloc_sbuf_tensor_at(name, shape, dt, offset=SBASE + off)
    XT = [A(f"XT{c}", [128, S], BF16, 8192 * c) for c in range(8)]
    WQq = A("WQq", [128, 2048], BF16, 65536)
    WKq = A("WKq", [128, 2048], BF16, 69632)
    WVq = A("WVq", [128, 2048], BF16, 73728)
    KTq = [A(f"KTq{p}", [128, S], BF16, 77824 + 8192 * p) for p in range(2)]
    QTq = [A(f"QTq{p}", [128, T], BF16, 94208 + 4096 * p) for p in range(2)]
    Vq = [A(f"Vq{t}", [128, 256], BF16, 102400 + 512 * t) for t in range(32)]
    SCB = [A(f"SCB{g}", [128, 2048], BF16, 118784 + 4096 * g) for g in range(8)]
    Kf32 = A("Kf32", [128, S], F32, 118784)
    AOT = [A(f"AOT{c}", [128, T], BF16, 151552 + 4096 * c) for c in range(8)]
    av_sb = A("av_sb", [128, 512], F32, 184320)
    rv = A("rv", [65, 512], F32, 186368)
    rvh = A("rvh", [65, 512], BF16, 188416)
    rvl = A("rvl", [65, 512], BF16, 189440)
    rvd = A("rvd", [65, 512], F32, 190464)
    Wo_sb = [A(f"Wo{c}", [128, 1024], BF16, 2048 * c) for c in range(8)]
    o_sb = [A(f"osb{i}", [128, 512], F32, 16384 + 2048 * i) for i in range(8)]
    xres = [A(f"xres{i}", [128, 1024], F32, 32768 + 4096 * i) for i in range(2)]
    xpre = [A(f"xpre{i}", [128, 1024], F32, 40960 + 4096 * i) for i in range(2)]
    sq1 = A("sq1", [128, 1024], F32, 49152)
    x1t = [A(f"x1t{i}", [128, 1024], F32, 53248 + 4096 * i) for i in range(2)]
    x1tb = [A(f"x1tb{i}", [128, 1024], BF16, 61440 + 2048 * i) for i in range(2)]
    X1F = [A(f"X1F{c}", [128, T], BF16, 4096 * c) for c in range(8)]
    WGT = [A(f"WGT{c}", [128, HID], BF16, 32768 + 6144 * c) for c in range(8)]
    WUT = [A(f"WUT{c}", [128, HID], BF16, 81920 + 6144 * c) for c in range(8)]
    WDT = [A(f"WDT{j}", [128, 1024], BF16, 131072 + 2048 * j) for j in range(24)]
    s_sb = A("s_sb", [128, 256], BF16, 180224)
    h_sb = [A(f"hsb{i}", [128, 256], BF16, 180736 + 512 * i) for i in range(2)]
    dn_sb = A("dn_sb", [128, 2048], F32, 181760)
    x2pre = [A(f"x2pre{i}", [128, 1024], F32, 189952 + 4096 * i) for i in range(2)]
    sq2 = A("sq2", [128, 1024], F32, 198144)
    x1re = A("x1re", [128, 1024], F32, 202240)
    P = lambda name, shape, dt: nc.alloc_sbuf_tensor(name, shape, dt, side="right")
    ident = P("ident", [128, 128], F32)
    ones_col = P("ones_col", [128, 1], BF16)
    ones_bc = P("ones_bc", [65, 64], BF16)
    ones_row = P("ones_row", [1, 128], BF16)
    zeros_row = P("zeros_row", [1, 128], BF16)
    bvrow = P("bvrow", [1, 1024], BF16)
    bq_sb = P("bq_sb", [128, 8], F32)
    bk_sb = P("bk_sb", [128, 8], F32)
    bo_sb = P("bo_sb", [128, 8], F32)
    ln_sum = P("ln_sum", [128, 1], F32)
    ln_ssq = P("ln_ssq", [128, 1], F32)
    ln_mean = P("ln_mean", [128, 1], F32)
    ln_var = P("ln_var", [128, 1], F32)
    ln_m2 = P("ln_m2", [128, 1], F32)
    ln_sd = P("ln_sd", [128, 1], F32)
    ln_rstd = P("ln_rstd", [128, 1], F32)
    kmean = P("kmean", [128, 1], F32)

    ps_big = nc.alloc_psum_tensor("ps_big", [128, 2048], F32)
    ps_b5 = nc.alloc_psum_tensor("ps_b5", [128, 512], F32)
    ps_bc = nc.alloc_psum_tensor("ps_bc", [128, 512], F32)
    ps_tok = nc.alloc_psum_tensor("ps_tok", [128, 1024], F32)

    sc = Sched(limit)
    st = sc.step

    m_xcast = st("gpsimd", lambda e, M: e.dma_start(out=s_xbf, in_=x_in), inc="dg")
    m_wgc = st("gpsimd", lambda e, M: e.dma_start(out=s_wg, in_=wg), inc="dg")
    m_wuc = st("gpsimd", lambda e, M: e.dma_start(out=s_wu, in_=wu), inc="dg")
    m_wdc = st("gpsimd", lambda e, M: e.dma_start(out=s_wd, in_=wd), inc="dg")
    m_bv = st("gpsimd", lambda e, M: e.dma_start(out=bvrow[0:1, :], in_=bv[None, :]), inc="dg")
    st("sync", lambda e, M: e.dma_start(out=ident[:], in_=ident_dram.ap()), inc="ds")
    for c in range(8):
        st("sync", lambda e, M, c=c: e.dma_start(out=bq_sb[:, c], in_=bq[c * 128:(c + 1) * 128]), inc="ds")
        st("sync", lambda e, M, c=c: e.dma_start(out=bk_sb[:, c], in_=bk[c * 128:(c + 1) * 128]), inc="ds")
        st("sync", lambda e, M, c=c: e.dma_start(out=bo_sb[:, c], in_=bo[c * 128:(c + 1) * 128]), inc="ds")
    m_bias = sc.n["ds"]
    st("vector", lambda e, M: e.memset(ones_col[:], 1.0), inc="dve")
    st("vector", lambda e, M: e.memset(ones_bc[:], 1.0), inc="dve")
    st("vector", lambda e, M: e.memset(ones_row[:], 1.0), inc="dve")
    m_ones = st("vector", lambda e, M: e.memset(zeros_row[:], 0.0), inc="dve")
    m_xt = 0
    for c in range(8):
        m_xt = st("sync", lambda e, M, c=c: e.dma_start_transpose(XT[c][:], s_xbf[:, c * 128:(c + 1) * 128]),
                  waits=[("dg", m_xcast)], inc="ds")

    quarter_free = [0, 0, 0, 0]
    m_exp = [0] * 8
    m_b5_free = []
    m_bc_read = 0
    m_bc_done = 0
    m_avsb = 0
    m_mult = 0
    m_av_done = 0
    m_kcent_all = 0
    m_vq = [0] * 32
    m_wslice_read = 0
    m_rvchain = 0

    for g in range(n_quads):
        cols = slice(g * 256, (g + 1) * 256)
        m_wqs = st("gpsimd", lambda e, M, v=wq.rearrange("(c p) n -> p c n", p=128)[:, :, cols]:
                   e.dma_start(out=WQq.ap().rearrange("p (c n) -> p c n", c=8), in_=v),
                   waits=[("pe", m_wslice_read)], inc="dg")
        m_wks = st("gpsimd", lambda e, M, v=wk.rearrange("(c p) n -> p c n", p=128)[:, :, cols]:
                   e.dma_start(out=WKq.ap().rearrange("p (c n) -> p c n", c=8), in_=v),
                   waits=[("pe", m_wslice_read)], inc="dg")
        m_wvs = st("gpsimd", lambda e, M, v=wv.rearrange("(c p) n -> p c n", p=128)[:, :, cols]:
                   e.dma_start(out=WVq.ap().rearrange("p (c n) -> p c n", c=8), in_=v),
                   waits=[("pe", m_wslice_read)], inc="dg")

        m_kcent = [0, 0]
        for pc in range(2):
            m_kevac = [0] * 8
            for tcol in range(8):
                qb = tcol % 4
                w0 = [("ds", m_xt), ("dg", m_wks), ("act", quarter_free[qb]),
                      ("pe", m_av_done), ("dve", m_mult)]
                mm_last = None
                for ci in range(8):
                    def mmk(e, M, ci=ci, pc=pc, tcol=tcol, qb=qb):
                        return e.matmul(ps_big[:, qb * 512:qb * 512 + 512],
                                        WKq[:, ci * 256 + pc * 128: ci * 256 + pc * 128 + 128],
                                        XT[ci][:, tcol * 512:(tcol + 1) * 512],
                                        start=(ci == 0), stop=(ci == 7))
                    mm_last = st("tensor", mmk, waits=(w0 if ci == 0 else ()), inc=("pe" if ci == 7 else None))
                def kev(e, M, pc=pc, tcol=tcol, qb=qb, g=g):
                    return e.activation(Kf32[:, tcol * 512:(tcol + 1) * 512], ps_big[:, qb * 512:qb * 512 + 512],
                                        AF.Identity, bias=bk_sb[:, 2 * g + pc:2 * g + pc + 1])
                m_kevac[tcol] = st("scalar", kev,
                                   waits=[("pe", mm_last), ("ds", m_bias), ("pe", m_av_done), ("dve", m_mult),
                                          ("dve", m_kcent[0] if pc == 1 else 0)],
                                   inc="act")
                quarter_free[qb] = m_kevac[tcol]
            m_red = st("vector", lambda e, M: e.tensor_reduce(kmean[:], Kf32[:], AX.X, OP.add),
                       waits=[("act", m_kevac[7]), ("dve", m_kcent_all)], inc="dve")
            m_scl = st("vector", lambda e, M: e.tensor_scalar(kmean[:], kmean[:], 1.0 / S, None, OP.mult),
                       waits=[("dve", m_red)], inc="dve")
            m_kcent[pc] = st("vector", lambda e, M, pc=pc: e.tensor_scalar(
                KTq[pc][:], Kf32[:], kmean[:, 0:1], None, OP.subtract),
                waits=[("dve", m_scl), ("pe", m_av_done)], inc="dve")
        m_kcent_all = m_kcent[1]

        m_qevac = [[0] * NQT, [0] * NQT]
        for pc in range(2):
            for qt in range(NQT):
                qb = qt % 4
                w0 = [("act", quarter_free[qb]), ("dg", m_wqs)]
                mm_last = None
                for ci in range(8):
                    def mmq(e, M, ci=ci, pc=pc, qt=qt, qb=qb):
                        return e.matmul(ps_big[:, qb * 512:qb * 512 + 512],
                                        WQq[:, ci * 256 + pc * 128: ci * 256 + pc * 128 + 128],
                                        XT[ci][:, qt * 512:(qt + 1) * 512],
                                        start=(ci == 0), stop=(ci == 7))
                    mm_last = st("tensor", mmq, waits=(w0 if ci == 0 else ()), inc=("pe" if ci == 7 else None))
                def qev(e, M, pc=pc, qt=qt, qb=qb, g=g):
                    return e.activation(QTq[pc][:, qt * 512:(qt + 1) * 512], ps_big[:, qb * 512:qb * 512 + 512],
                                        AF.Identity, bias=bq_sb[:, 2 * g + pc:2 * g + pc + 1])
                m_qevac[pc][qt] = st("scalar", qev, waits=[("pe", mm_last)], inc="act")
                quarter_free[qb] = m_qevac[pc][qt]

        for tt in range(32):
            pjv = ps_b5 if tt % 2 == 0 else ps_bc
            w0 = [("dg", m_wvs), ("dve", m_ones), ("dg", m_bv)]
            if tt >= 2:
                w0.append(("dve", m_vq[tt - 2]))
            else:
                w0 += m_b5_free
            mm_last = None
            for ci in range(8):
                def mmv(e, M, ci=ci, tt=tt, pjv=pjv):
                    return e.matmul(pjv[:, 0:256],
                                    XT[ci][:, tt * 128:(tt + 1) * 128],
                                    WVq[:, ci * 256:(ci + 1) * 256],
                                    start=(ci == 0), stop=False)
                st("tensor", mmv, waits=(w0 if ci == 0 else ()))
            m_vmm = st("tensor", lambda e, M, pjv=pjv, g=g: e.matmul(
                pjv[:, 0:256], ones_row[:], bvrow[0:1, g * 256:(g + 1) * 256],
                start=False, stop=True), inc="pe")
            m_vq[tt] = st("vector", lambda e, M, tt=tt, pjv=pjv: e.tensor_copy(
                Vq[tt][:], pjv[:, 0:256]), waits=[("pe", m_vmm)], inc="dve")
        m_wslice_read = sc.n["pe"]
        m_b5_free = [("dve", m_vq[30]), ("dve", m_vq[31])]

        for h4 in range(n_heads):
            par = h4 % 2
            pc = h4 // 2
            base = 64 * par
            ch = 2 * g + pc
            drow = 64 if par == 0 else 32
            for qt in range(NQT):
                for cg in range(8):
                    w0 = [("dve", m_kcent[pc]), ("act", m_qevac[pc][qt]),
                          ("act", m_exp[cg - 1]), ("pe", m_av_done)]
                    w0 += [("act", quarter_free[j]) for j in range(4)]
                    mm_last = None
                    for j in range(4):
                        kc = 4 * cg + j
                        def mms(e, M, j=j, kc=kc, pc=pc, base=base, qt=qt):
                            return e.matmul(ps_big[:, j * 512:(j + 1) * 512],
                                            KTq[pc][base:base + 64, kc * 128:(kc + 1) * 128],
                                            QTq[pc][base:base + 64, qt * 512:(qt + 1) * 512],
                                            start=True, stop=True, tile_position=(base, 0))
                        mm_last = st("tensor", mms, waits=(w0 if j == 0 else ()), inc=("pe" if j == 3 else None))
                    m_exp[cg] = st("scalar", lambda e, M, cg=cg: e.activation(
                        SCB[cg][:], ps_big[:], AF.Exp, scale=0.125),
                        waits=[("pe", mm_last), ("dve", m_kcent_all), ("dve", m_mult)], inc="act")
                quarter_free = [m_exp[cg]] * 4
                w_first = [("act", m_exp[7]), ("act", m_avsb), ("dve", m_rvchain)] + m_b5_free
                m_av = None
                for kc in range(32):
                    strt, stop = (kc == 0), (kc == 31)
                    sl = SCB[kc // 4][:, (kc % 4) * 512:(kc % 4) * 512 + 512]
                    if par == 0:
                        def mma(e, M, kc=kc, h4=h4, sl=sl, strt=strt, stop=stop):
                            return e.matmul(ps_b5[0:64, :], Vq[kc][:, h4 * 64:(h4 + 1) * 64], sl,
                                            start=strt, stop=stop, tile_position=(0, 0))
                        def mmd(e, M, sl=sl, strt=strt, stop=stop):
                            return e.matmul(ps_b5[64:65, :], ones_col[:], sl,
                                            start=strt, stop=stop, tile_position=(0, 64))
                    else:
                        def mma(e, M, kc=kc, h4=h4, sl=sl, strt=strt, stop=stop):
                            return e.matmul(ps_b5[64:128, :], Vq[kc][:, h4 * 64:(h4 + 1) * 64], sl,
                                            start=strt, stop=stop, tile_position=(0, 64))
                        def mmd(e, M, sl=sl, strt=strt, stop=stop):
                            return e.matmul(ps_b5[32:33, :], ones_col[:], sl,
                                            start=strt, stop=stop, tile_position=(0, 32))
                    st("tensor", mma, waits=(w_first if kc == 0 else ()))
                    m_av = st("tensor", mmd, inc=("pe" if stop else None))
                m_av_done = m_av
                m_r1 = st("vector", lambda e, M, drow=drow: e.reciprocal(
                    rv[drow:drow + 1, :], ps_b5[drow:drow + 1, :]),
                    waits=[("pe", m_av), ("pe", m_bc_done)], inc="dve")
                m_r2 = st("vector", lambda e, M, drow=drow: e.tensor_copy(
                    rvh[drow:drow + 1, :], rv[drow:drow + 1, :]), waits=[("dve", m_r1)], inc="dve")
                m_r3 = st("vector", lambda e, M, drow=drow: e.tensor_tensor(
                    rvd[drow:drow + 1, :], rv[drow:drow + 1, :], rvh[drow:drow + 1, :], OP.subtract),
                    waits=[("dve", m_r2)], inc="dve")
                m_r4 = st("vector", lambda e, M, drow=drow: e.tensor_copy(
                    rvl[drow:drow + 1, :], rvd[drow:drow + 1, :]), waits=[("dve", m_r3)], inc="dve")
                m_rvchain = m_r4
                m_avsb = st("scalar", lambda e, M, base=base: e.copy(
                    av_sb[base:base + 64, :], ps_b5[base:base + 64, :]),
                    waits=[("pe", m_av), ("dve", m_mult)], inc="act")
                tp = (drow, 0) if par == 0 else (drow, 64)
                ob = 0 if par == 0 else 64
                st("tensor", lambda e, M, drow=drow, ob=ob, tp=tp: e.matmul(
                    ps_bc[ob:ob + 64, :], ones_bc[drow:drow + 1, 0:64], rvh[drow:drow + 1, :],
                    start=True, stop=False, tile_position=tp),
                   waits=[("dve", m_r4), ("dve", m_bc_read)])
                m_bc_done = st("tensor", lambda e, M, drow=drow, ob=ob, tp=tp: e.matmul(
                    ps_bc[ob:ob + 64, :], ones_bc[drow:drow + 1, 0:64], rvl[drow:drow + 1, :],
                    start=False, stop=True, tile_position=tp), inc="pe")
                m_mult = st("vector", lambda e, M, ch=ch, base=base, qt=qt: e.tensor_tensor(
                    AOT[ch][base:base + 64, qt * 512:(qt + 1) * 512],
                    av_sb[base:base + 64, :], ps_bc[base:base + 64, :], OP.mult),
                    waits=[("pe", m_bc_done), ("act", m_avsb)], inc="dve")
                m_bc_read = m_mult
                m_b5_free = [("dve", m_r1), ("act", m_avsb), ("dve", m_bc_read)]

    p2_pe_done = sc.n["pe"]
    p2_mult_done = m_mult
    p2_avsb_done = m_avsb

    if not do_p3:
        do_p4 = False
    m_wo = 0
    for c in (range(8) if do_p3 else range(0)):
        m_wo = st("gpsimd", lambda e, M, c=c: e.dma_start(
            out=Wo_sb[c].ap(), in_=wo.rearrange("(c p) n -> p c n", p=128)[:, c, :]),
            waits=[("pe", p2_pe_done)], inc="dg")
    m_oev = [0] * 8
    m_tr_prev = 0
    m_xadd = [0, 0]
    m_sq_prev = 0
    m_ln_prev = 0
    m_var0_prev = 0
    m_x1store = [0, 0]
    m_xres_ld = [0, 0]
    m_ps_tok_free = 0
    for qt in (range(4) if do_p3 else range(0)):
        for outc in range(8):
            ps_o = ps_b5 if outc % 2 == 0 else ps_bc
            w0 = [("dg", m_wo), ("dve", p2_mult_done), ("act", m_oev[outc - 2] if outc >= 2 else 0),
                  ("pe", m_tr_prev)]
            if qt == 0 and outc < 2:
                w0 += [("dve", m_r1), ("act", p2_avsb_done), ("dve", m_bc_read)]
            mm_last = None
            for ci in range(8):
                def mmo(e, M, ci=ci, outc=outc, qt=qt, ps_o=ps_o):
                    return e.matmul(ps_o[:], Wo_sb[ci][:, outc * 128:(outc + 1) * 128],
                                    AOT[ci][:, qt * 512:(qt + 1) * 512],
                                    start=(ci == 0), stop=(ci == 7))
                mm_last = st("tensor", mmo, waits=(w0 if ci == 0 else ()), inc=("pe" if ci == 7 else None))
            m_oev[outc] = st("scalar", lambda e, M, outc=outc, ps_o=ps_o: e.activation(
                o_sb[outc][:], ps_o[:], AF.Identity, bias=bo_sb[:, outc:outc + 1]),
                waits=[("pe", mm_last), ("pe", m_tr_prev)], inc="act")
        for qs in range(4):
            i2 = qs % 2
            tok0 = qt * 512 + qs * 128
            m_xres_ld[i2] = st("sync", lambda e, M, i2=i2, tok0=tok0: e.dma_start(
                out=xres[i2][:], in_=x_in[tok0:tok0 + 128, :]),
                waits=[("dve", m_xadd[i2]), ("pe", p2_pe_done)], inc="ds")
            m_tr = None
            for outc in range(8):
                def trp(e, M, outc=outc, qs=qs):
                    return e.transpose(ps_tok[:, outc * 128:(outc + 1) * 128],
                                      o_sb[outc][:, qs * 128:(qs + 1) * 128], ident[:])
                m_tr = st("tensor", trp,
                          waits=([("act", m_oev[7]), ("dve", m_ps_tok_free)] if outc == 0 else ()),
                          inc=("pe" if outc == 7 else None))
            m_tr_prev = m_tr
            m_add = st("vector", lambda e, M, i2=i2: e.tensor_tensor(
                xpre[i2][:], ps_tok[:], xres[i2][:], OP.add),
                waits=[("pe", m_tr), ("ds", m_xres_ld[i2]), ("act", m_sq_prev), ("dve", m_ln_prev)], inc="dve")
            m_ps_tok_free = m_add
            m_xadd[i2] = m_add
            m_red = st("vector", lambda e, M, i2=i2: e.tensor_reduce(
                ln_sum[:], xpre[i2][:], AX.X, OP.add), waits=[("dve", m_add), ("dve", m_ln_prev)], inc="dve")
            m_sq = st("scalar", lambda e, M, i2=i2: e.activation(
                sq1[:], xpre[i2][:], AF.Square, accum_out=ln_ssq[:, 0:1]),
                waits=[("dve", m_add), ("dve", m_var0_prev)], inc="act")
            m_sq_prev = m_sq
            m_mean = st("vector", lambda e, M: e.tensor_scalar(
                ln_mean[:], ln_sum[:], 1.0 / D, None, OP.mult), waits=[("dve", m_red)], inc="dve")
            m_m2 = st("vector", lambda e, M: e.tensor_tensor(
                ln_m2[:], ln_mean[:], ln_mean[:], OP.mult), waits=[("dve", m_mean)], inc="dve")
            m_var0 = st("vector", lambda e, M: e.tensor_scalar(
                ln_var[:], ln_ssq[:], 1.0 / D, EPS, OP.mult, OP.add),
                waits=[("act", m_sq), ("dve", m_m2)], inc="dve")
            m_var0_prev = m_var0
            m_var = st("vector", lambda e, M: e.tensor_tensor(
                ln_var[:], ln_var[:], ln_m2[:], OP.subtract), waits=[("dve", m_var0)], inc="dve")
            m_sd = st("scalar", lambda e, M: e.activation(
                ln_sd[:], ln_var[:], AF.Sqrt), waits=[("dve", m_var)], inc="act")
            m_rstd = st("vector", lambda e, M: e.reciprocal(
                ln_rstd[:], ln_sd[:]), waits=[("act", m_sd)], inc="dve")
            m_app = st("vector", lambda e, M, i2=i2: e.tensor_scalar(
                x1t[i2][:], xpre[i2][:], ln_mean[:, 0:1], ln_rstd[:, 0:1], OP.subtract, OP.mult),
                waits=[("dve", m_rstd), ("dg", m_x1store[i2])], inc="dve")
            m_cast = st("vector", lambda e, M, i2=i2: e.tensor_copy(
                x1tb[i2][:], x1t[i2][:]), waits=[("dve", m_app)], inc="dve")
            m_ln_prev = m_app
            st("gpsimd", lambda e, M, i2=i2, tok0=tok0: e.dma_start(
                out=s_x1f[tok0:tok0 + 128, :], in_=x1t[i2][:]), waits=[("dve", m_app)], inc="dg")
            m_x1store[i2] = st("gpsimd", lambda e, M, i2=i2, tok0=tok0: e.dma_start(
                out=s_x1b[tok0:tok0 + 128, :], in_=x1tb[i2][:]), waits=[("dve", m_cast)], inc="dg")

    p3_pe = sc.n["pe"]
    p3_act = sc.n["act"]
    p3_dve = sc.n["dve"]
    p3_dg = sc.n["dg"]
    R8 = range(8) if do_p4 else range(0)
    R24 = range(24) if do_p4 else range(0)

    p3w = [("pe", p3_pe), ("act", p3_act), ("dve", p3_dve), ("dg", p3_dg)]
    m_x1f = 0
    for c in R8:
        m_x1f = st("sync", lambda e, M, c=c: e.dma_start_transpose(
            X1F[c][:], s_x1b[:, c * 128:(c + 1) * 128]), waits=p3w, inc="ds")
    m_wgt = 0
    for c in R8:
        m_wgt = st("sync", lambda e, M, c=c: e.dma_start_transpose(
            WGT[c][:], s_wg[:, c * 128:(c + 1) * 128]), waits=p3w + [("dg", m_wgc)], inc="ds")
    m_wut = 0
    for c in R8:
        m_wut = st("sync", lambda e, M, c=c: e.dma_start_transpose(
            WUT[c][:], s_wu[:, c * 128:(c + 1) * 128]), waits=p3w + [("dg", m_wuc)], inc="ds")
    m_wdt = 0
    for j in R24:
        m_wdt = st("sync", lambda e, M, j=j: e.dma_start_transpose(
            WDT[j][:], s_wd[:, j * 128:(j + 1) * 128]), waits=p3w + [("dg", m_wdc)], inc="ds")

    m_si = [0] * 24
    m_h = [0] * 24
    m_dnj = [0] * 24
    m_dnev = [0] * 8
    m_x2add = [0, 0]
    m_x2store = [0, 0]
    m_tr2_prev = 0
    m_x1re_read = 0
    m_dnev_last = 0
    for qt in (range(8) if do_p4 else range(0)):
        for j in range(24):
            pj = ps_b5 if j % 2 == 0 else ps_bc
            w0 = [("ds", m_wgt), ("ds", m_wut), ("ds", m_x1f)]
            if j >= 2:
                w0 += [("act", m_si[j - 2]), ("dve", m_h[j - 2])]
            elif qt > 0:
                w0 += [("act", m_si[22 + j]), ("dve", m_h[22 + j])]
            else:
                w0 += [("dve", m_r1), ("act", p2_avsb_done), ("dve", m_bc_read)]
            mm_last = None
            for ci in range(8):
                def mmg(e, M, ci=ci, j=j, qt=qt, pj=pj):
                    return e.matmul(pj[:, 0:256], WGT[ci][:, j * 128:(j + 1) * 128],
                                    X1F[ci][:, qt * 256:(qt + 1) * 256],
                                    start=(ci == 0), stop=(ci == 7))
                st("tensor", mmg, waits=(w0 if ci == 0 else ()))
            for ci in range(8):
                def mmu(e, M, ci=ci, j=j, qt=qt, pj=pj):
                    return e.matmul(pj[:, 256:512], WUT[ci][:, j * 128:(j + 1) * 128],
                                    X1F[ci][:, qt * 256:(qt + 1) * 256],
                                    start=(ci == 0), stop=(ci == 7))
                mm_last = st("tensor", mmu, inc=("pe" if ci == 7 else None))
            m_si[j] = st("scalar", lambda e, M, pj=pj: e.activation(
                s_sb[:], pj[:, 0:256], AF.Silu, scale=DEQ),
                waits=[("pe", mm_last)] + ([("dve", m_h[j - 1])] if j >= 1 else
                                           ([("dve", m_h[23])] if qt > 0 else [])), inc="act")
            m_h[j] = st("vector", lambda e, M, j=j, pj=pj: e.tensor_tensor(
                h_sb[j % 2][:], s_sb[:], pj[:, 256:512], OP.mult),
                waits=[("act", m_si[j])] + ([("pe", m_dnj[j - 2])] if j >= 2 else
                                            ([("pe", m_dnj[22 + j])] if qt > 0 else [])), inc="dve")
            w_dn = [("dve", m_h[j]), ("ds", m_wdt)]
            if j == 0:
                w_dn += [("dve", m_x2add[0]), ("dve", m_x2add[1]), ("act", m_dnev_last)]
                for bank in range(4):
                    def mmz(e, M, bank=bank):
                        return e.matmul(ps_big[:, bank * 512:(bank + 1) * 512],
                                        zeros_row[:], bvrow[0:1, 0:512], start=True, stop=False)
                    st("tensor", mmz, waits=(w_dn if bank == 0 else ()))
                w_dn = []
            mm_last = None
            for outc in range(8):
                def mmdn(e, M, outc=outc, j=j):
                    return e.matmul(ps_big[:, outc * 256:(outc + 1) * 256],
                                    WDT[j][:, outc * 128:(outc + 1) * 128], h_sb[j % 2][:],
                                    start=False, stop=(j == 23))
                mm_last = st("tensor", mmdn, waits=(w_dn if outc == 0 else ()), inc=("pe" if outc == 7 else None))
            m_dnj[j] = mm_last
        for outc in range(8):
            m_dnev[outc] = st("scalar", lambda e, M, outc=outc: e.activation(
                dn_sb[:, outc * 256:(outc + 1) * 256], ps_big[:, outc * 256:(outc + 1) * 256],
                AF.Identity, scale=DEQ2),
                waits=[("pe", m_dnj[23]), ("pe", m_tr2_prev)], inc="act")
        m_dnev_last = m_dnev[7]
        for qs in range(2):
            i2 = qs % 2
            tok0 = qt * 256 + qs * 128
            m_ld = st("sync", lambda e, M, tok0=tok0: e.dma_start(
                out=x1re[:], in_=s_x1f[tok0:tok0 + 128, :]),
                waits=[("dve", m_x1re_read), ("dg", p3_dg)], inc="ds")
            m_tr = None
            for outc in range(8):
                def trp2(e, M, outc=outc, qs=qs):
                    return e.transpose(ps_big[:, qs * 1024 + outc * 128: qs * 1024 + (outc + 1) * 128],
                                      dn_sb[:, outc * 256 + qs * 128: outc * 256 + (qs + 1) * 128], ident[:])
                m_tr = st("tensor", trp2,
                          waits=([("act", m_dnev[7]), ("dve", m_x2add[i2])] if outc == 0 else ()),
                          inc=("pe" if outc == 7 else None))
            m_tr2_prev = m_tr
            m_add = st("vector", lambda e, M, i2=i2, qs=qs: e.tensor_tensor(
                x2pre[i2][:], ps_big[:, qs * 1024:(qs + 1) * 1024], x1re[:], OP.add),
                waits=[("pe", m_tr), ("ds", m_ld), ("dg", m_x2store[i2]),
                       ("act", m_sq_prev), ("dve", m_ln_prev)], inc="dve")
            m_x1re_read = m_add
            m_x2add[i2] = m_add
            m_red = st("vector", lambda e, M, i2=i2: e.tensor_reduce(
                ln_sum[:], x2pre[i2][:], AX.X, OP.add), waits=[("dve", m_add), ("dve", m_ln_prev)], inc="dve")
            m_sq = st("scalar", lambda e, M, i2=i2: e.activation(
                sq2[:], x2pre[i2][:], AF.Square, accum_out=ln_ssq[:, 0:1]),
                waits=[("dve", m_add), ("dve", m_var0_prev)], inc="act")
            m_sq_prev = m_sq
            m_mean = st("vector", lambda e, M: e.tensor_scalar(
                ln_mean[:], ln_sum[:], 1.0 / D, None, OP.mult), waits=[("dve", m_red)], inc="dve")
            m_m2 = st("vector", lambda e, M: e.tensor_tensor(
                ln_m2[:], ln_mean[:], ln_mean[:], OP.mult), waits=[("dve", m_mean)], inc="dve")
            m_var0 = st("vector", lambda e, M: e.tensor_scalar(
                ln_var[:], ln_ssq[:], 1.0 / D, EPS, OP.mult, OP.add),
                waits=[("act", m_sq), ("dve", m_m2)], inc="dve")
            m_var0_prev = m_var0
            m_var = st("vector", lambda e, M: e.tensor_tensor(
                ln_var[:], ln_var[:], ln_m2[:], OP.subtract), waits=[("dve", m_var0)], inc="dve")
            m_sd = st("scalar", lambda e, M: e.activation(
                ln_sd[:], ln_var[:], AF.Sqrt), waits=[("dve", m_var)], inc="act")
            m_rstd = st("vector", lambda e, M: e.reciprocal(
                ln_rstd[:], ln_sd[:]), waits=[("act", m_sd)], inc="dve")
            m_app = st("vector", lambda e, M, i2=i2: e.tensor_scalar(
                x2pre[i2][:], x2pre[i2][:], ln_mean[:, 0:1], ln_rstd[:, 0:1], OP.subtract, OP.mult),
                waits=[("dve", m_rstd)], inc="dve")
            m_ln_prev = m_app
            m_x2store[i2] = st("gpsimd", lambda e, M, i2=i2, tok0=tok0: e.dma_start(
                out=out[tok0:tok0 + 128, :], in_=x2pre[i2][:]), waits=[("dve", m_app)], inc="dg")

    final_dg = sc.n["dg"]
    sc.q["gpsimd"].append(lambda e, SM: e.wait_ge(SM["dg"], final_dg))
    sc.meta["gpsimd"].append(((("dg", final_dg),), None, "final"))
    sc.check()

    from contextlib import ExitStack
    es = ExitStack()
    SM = {s: es.enter_context(nc.semaphore(s)) for s in ("ds", "dg", "pe", "act", "dve")}
    block = es.enter_context(nc.Block())

    @block.sync
    def _(e):
        for f in sc.q["sync"]:
            f(e, SM)

    @block.gpsimd
    def _(e):
        for f in sc.q["gpsimd"]:
            f(e, SM)

    @block.tensor
    def _(e):
        for f in sc.q["tensor"]:
            f(e, SM)

    @block.scalar
    def _(e):
        for f in sc.q["scalar"]:
            f(e, SM)

    @block.vector
    def _(e):
        for f in sc.q["vector"]:
            f(e, SM)

    es.close()
    return nc


_NC_CACHE = None


def kernel(**inputs):
    global _NC_CACHE
    X = np.asarray(inputs["L0_attn_X"], dtype=np.float32)
    args = {
        "wq": np.ascontiguousarray(inputs["L0_attn_Wq"], dtype=np.float32),
        "wk": np.ascontiguousarray(inputs["L0_attn_Wk"], dtype=np.float32),
        "wv": np.ascontiguousarray(inputs["L0_attn_Wv"], dtype=np.float32),
        "wo": np.ascontiguousarray(inputs["L0_attn_Wo"], dtype=np.float32),
        "bq": np.ascontiguousarray(inputs["L0_attn_bq"], dtype=np.float32),
        "bk": np.ascontiguousarray(inputs["L0_attn_bk"], dtype=np.float32),
        "bv": np.ascontiguousarray(inputs["L0_attn_bv"], dtype=np.float32),
        "bo": np.ascontiguousarray(inputs["L0_attn_bo"], dtype=np.float32),
        "wg": np.ascontiguousarray(inputs["L0_ffn_w_gate"], dtype=np.int32),
        "wu": np.ascontiguousarray(inputs["L0_ffn_w_up"], dtype=np.int32),
        "wd": np.ascontiguousarray(inputs["L0_ffn_w_down"], dtype=np.int32),
    }
    if _NC_CACHE is None:
        _NC_CACHE = build_nc()
    nc = _NC_CACHE
    in_maps = []
    for c in range(8):
        b, qh = c // 2, c % 2
        xb = X[b]
        if qh:
            xb = np.roll(xb, -T, axis=0)
        m = {"x_in": np.ascontiguousarray(xb)}
        m.update(args)
        in_maps.append(m)
    res = run_bass_kernel_spmd(nc, in_maps, core_ids=list(range(8)))
    outp = np.empty((B, S, D), dtype=np.float32)
    for c in range(8):
        b, qh = c // 2, c % 2
        outp[b, qh * T:(qh + 1) * T] = res.results[c]["out"]
    return outp


if __name__ == "__main__":
    rng = np.random.default_rng(0)
    ins = {
        "L0_attn_X": rng.standard_normal((B, S, D), dtype=np.float32),
        "L0_attn_Wq": rng.standard_normal((D, D), dtype=np.float32) / 32,
        "L0_attn_bq": rng.standard_normal(D, dtype=np.float32) * 0.1,
        "L0_attn_Wk": rng.standard_normal((D, D), dtype=np.float32) / 32,
        "L0_attn_bk": rng.standard_normal(D, dtype=np.float32) * 0.1,
        "L0_attn_Wv": rng.standard_normal((D, D), dtype=np.float32) / 32,
        "L0_attn_bv": rng.standard_normal(D, dtype=np.float32) * 0.1,
        "L0_attn_Wo": rng.standard_normal((D, D), dtype=np.float32) / 32,
        "L0_attn_bo": rng.standard_normal(D, dtype=np.float32) * 0.1,
        "L0_ffn_w_gate": rng.integers(-127, 128, (HID, D)).astype(np.int32),
        "L0_ffn_w_up": rng.integers(-127, 128, (HID, D)).astype(np.int32),
        "L0_ffn_w_down": rng.integers(-127, 128, (D, HID)).astype(np.int32),
    }
    y = kernel(**ins)
    print("out", y.shape, y.dtype, np.abs(y).max())
